# version 02
# baseline (speedup 1.0000x reference)
import numpy as np
import concourse.bass as bass
import concourse.mybir as mybir
from concourse.bass_utils import run_bass_kernel_spmd
from contextlib import ExitStack

F32 = mybir.dt.float32
BF16 = mybir.dt.bfloat16
I32 = mybir.dt.int32
AF = mybir.ActivationFunctionType
ALU = mybir.AluOpType
AX = mybir.AxisListType

NCORE = 8
S = 8192
E = 2048
NH = 16
HD = 128
ROT = 64
INTER = 5504
OC = 43
TOK = S // NCORE
NTT = TOK // 128
HC = E // 128
NST = S // 512
PIECES = [11, 11, 11, 10]
PSTART = [0, 11, 22, 33]
NLUT = 16
EXP_BIAS = -30.0
RG = [list(range(NCORE))]

STAGE = 6


def piece_of(oc):
    p = min(oc // 11, 3)
    return p, oc - PSTART[p]


class Ctr:
    def __init__(self, stack, nc, name):
        self.h = stack.enter_context(nc.semaphore(name))
        self.v = 0

    def inc(self, instr, n=1):
        instr.then_inc(self.h, n)
        self.v += n
        return instr


class Prog:

    def __init__(self, nc, stack):
        self.nc = nc
        self.sem = {}
        for name in ("ld", "ld2", "pe", "act", "dve", "cc", "out", "pl"):
            self.sem[name] = Ctr(stack, nc, "s_" + name)
        self.war = {}

    def wait(self, eng, sem_name, val=None):
        s = self.sem[sem_name]
        v = s.v if val is None else val
        if v > 0:
            eng.wait_ge(s.h, v)

    def mark(self, sem_name, instr, n=1):
        return self.sem[sem_name].inc(instr, n)

    def val(self, sem_name):
        return self.sem[sem_name].v

    def record_use(self, buf, sem_name):
        self.war[buf] = (sem_name, self.sem[sem_name].v)

    def wait_reuse(self, eng, buf):
        if buf in self.war:
            sem_name, v = self.war[buf]
            if v > 0:
                eng.wait_ge(self.sem[sem_name].h, v)

    def barrier(self, *engs):
        for eng in engs:
            for sem_name in ("pe", "act", "dve"):
                self.wait(eng, sem_name)


def build_program():
    nc = bass.Bass("TRN2", target_bir_lowering=False)

    x_d = nc.dram_tensor("x", [TOK, E], F32, kind="ExternalInput")
    gidxT_d = nc.dram_tensor("gidxT", [256, INTER], I32, kind="ExternalInput")
    uidxT_d = nc.dram_tensor("uidxT", [256, INTER], I32, kind="ExternalInput")
    didxT_d = nc.dram_tensor("didxT", [INTER, 256], I32, kind="ExternalInput")
    lut_d = nc.dram_tensor("lut", [128, NLUT], F32, kind="ExternalInput")
    wqkvT_d = nc.dram_tensor("wqkvT", [E, 768], F32, kind="ExternalInput")
    woutT_d = nc.dram_tensor("woutT", [E, E], F32, kind="ExternalInput")
    cossin_d = nc.dram_tensor("cossin", [2, 64, S], F32, kind="ExternalInput")
    masks_d = nc.dram_tensor("masks", [8, 128, 512], F32, kind="ExternalInput")
    out_d = nc.dram_tensor("out", [TOK, E], F32, kind="ExternalOutput")

    wpc = [nc.dram_tensor("wpc%d" % pc, [3, PIECES[pc], 128, 2, 128], BF16)
           for pc in range(4)]
    wag = [nc.dram_tensor("wag%d" % pc, [NCORE, 3, PIECES[pc], 128, 2, 128], BF16,
                          addr_space="Shared")
           for pc in range(4)]
    xnag_in = nc.dram_tensor("xnag_in", [E, TOK], BF16)
    xnag_out = nc.dram_tensor("xnag_out", [NCORE, E, TOK], BF16,
                              addr_space="Shared")
    qspill = nc.dram_tensor("qspill", [2, 128, S], BF16)
    kspill = nc.dram_tensor("kspill", [2, 128, S], BF16)
    vtspill = nc.dram_tensor("vtspill", [2, S, 128], BF16)
    avag_in = nc.dram_tensor("avag_in", [NCORE, 256, TOK], BF16)
    avag_out = nc.dram_tensor("avag_out", [NCORE, NCORE, 256, TOK], BF16,
                              addr_space="Shared")

    st = ExitStack()
    with st:
        blk = st.enter_context(nc.Block())
        p = Prog(nc, st)

        x_tm = st.enter_context(nc.sbuf_tensor("x_tm", [128, NTT * E], F32))
        ident = st.enter_context(nc.sbuf_tensor("ident_sb", [128, 128], BF16))
        identf = st.enter_context(nc.sbuf_tensor("identf_sb", [128, 128], F32))
        ones128 = st.enter_context(nc.sbuf_tensor("ones128", [128, 128], F32))
        xn_tb = [st.enter_context(nc.sbuf_tensor("xn_t%d" % i, [128, E], BF16))
                 for i in range(2)]
        ln_junk = st.enter_context(nc.sbuf_tensor("ln_junk", [128, E], BF16))
        ln_negm = st.enter_context(nc.sbuf_tensor("ln_negm", [128, NTT], F32))
        ln_vs = st.enter_context(nc.sbuf_tensor("ln_vs", [128, NTT], F32))
        ln_rs = st.enter_context(nc.sbuf_tensor("ln_rs", [128, NTT], F32))
        ln_t1 = st.enter_context(nc.sbuf_tensor("ln_t1", [128, 1], F32))
        o_stg = st.enter_context(nc.sbuf_tensor("o_stg", [128, E], F32))
        consts = st.enter_context(nc.sbuf_tensor("consts", [128, 2], F32))
        C_EPS, C_EB = 0, 1
        lutb = st.enter_context(nc.sbuf_tensor("lutb", [128, NLUT], F32))

        ps_tr = [st.enter_context(nc.psum_tensor("ps_tr%d" % i, [128, 512], BF16))
                 for i in range(2)]
        ps_a = st.enter_context(nc.psum_tensor("ps_a", [128, 512], F32))
        ps_b = st.enter_context(nc.psum_tensor("ps_b", [128, 512], F32))
        ps_c = [st.enter_context(nc.psum_tensor("ps_c%d" % i, [128, 512], F32))
                for i in range(2)]
        ps_av = st.enter_context(nc.psum_tensor("ps_av", [128, 512], F32))
        ps_zt = st.enter_context(nc.psum_tensor("ps_zt", [128, 512], BF16))

        def emit_prologue():
            nc.vector.memset(consts[:, C_EPS:C_EPS + 1], 1e-5)
            nc.vector.memset(ones128[:, :], 1.0)
            p.mark("dve", nc.vector.memset(consts[:, C_EB:C_EB + 1], EXP_BIAS))
            with nc.sbuf_tensor("iota_i32", [128, 128], I32) as iota_i32:
                p.mark("pl", nc.gpsimd.iota(iota_i32[:, :], pattern=[[1, 128]],
                                            base=0, channel_multiplier=-1))
                p.wait(nc.vector, "pl")
                nc.vector.tensor_scalar(
                    ident[:, :], iota_i32[:, :], 0.0, None, ALU.is_equal)
                p.mark("dve", nc.vector.tensor_scalar(
                    identf[:, :], iota_i32[:, :], 0.0, None, ALU.is_equal))
            p.mark("ld", nc.sync.dma_start(out=lutb[:, :], in_=lut_d.ap()), 16)
            xin = x_d.ap().rearrange("(tt p) d -> tt p d", p=128).transpose([1, 0, 2])
            xout_ap = x_tm.ap().rearrange("p (tt d) -> p tt d", tt=NTT)
            p.mark("ld", nc.sync.dma_start(out=xout_ap, in_=xin), 16)

        def emit_gather():
            with ExitStack() as gst:
                gi = gst.enter_context(nc.sbuf_tensor("gth_i32", [128, 11 * 256], I32))
                gb = gst.enter_context(nc.sbuf_tensor("gth_b", [128, 11 * 256], BF16))
                gacc = gst.enter_context(
                    nc.sbuf_tensor("gth_acc", [128, 11 * 256], BF16))
                gscr = gst.enter_context(
                    nc.sbuf_tensor("gth_scr", [128, 11 * 256], BF16))
                p.wait(nc.vector, "ld")

                def gather_tile(n_el):
                    p.wait(nc.vector, "ld")
                    nc.vector.tensor_copy(gb[:, :n_el], gi[:, :n_el])
                    nc.vector.tensor_scalar(gacc[:, :n_el], gb[:, :n_el],
                                            0.0, lutb[:, 0:1], ALU.is_equal,
                                            ALU.mult)
                    for v in range(1, NLUT):
                        nc.vector.tensor_scalar(gscr[:, :n_el], gb[:, :n_el],
                                                float(v), lutb[:, v:v + 1],
                                                ALU.is_equal, ALU.mult)
                        ins = nc.vector.tensor_tensor(
                            gacc[:, :n_el], gacc[:, :n_el], gscr[:, :n_el], ALU.add)
                    p.mark("dve", ins)

                for pc in range(4):
                    npc = PIECES[pc]
                    o0 = PSTART[pc] * 128
                    o1 = o0 + npc * 128
                    for mat, src in ((0, gidxT_d), (1, uidxT_d)):
                        for ht in range(2):
                            p.wait_reuse(nc.sync, "gth")
                            p.mark("ld", nc.sync.dma_start(
                                out=gi[:, :npc * 128],
                                in_=src.ap()[128 * ht:128 * (ht + 1), o0:o1]), 16)
                            gather_tile(npc * 128)
                            acc_ap = gacc.ap()[:, :npc * 128] \
                                .rearrange("p (c o) -> p c o", c=npc)
                            wout = wpc[pc].ap()[mat, :, :, ht, :] \
                                .transpose([1, 0, 2])
                            p.wait(nc.sync, "dve")
                            p.mark("ld", nc.sync.dma_start(
                                out=wout, in_=acc_ap), 16)
                            p.record_use("gth", "ld")
                    p.wait_reuse(nc.sync, "gth")
                    din = didxT_d.ap()[o0:o1, :] \
                        .rearrange("(c p) d -> c p d", p=128).transpose([1, 0, 2])
                    gout = gi.ap()[:, :npc * 256].rearrange("p (c d) -> p c d", c=npc)
                    p.mark("ld", nc.sync.dma_start(out=gout, in_=din), 16)
                    gather_tile(npc * 256)
                    acc_ap = gacc.ap()[:, :npc * 256] \
                        .rearrange("p (c td) -> p c td", c=npc)
                    wout = wpc[pc].ap()[2].rearrange("c p t d -> c p (t d)") \
                        .transpose([1, 0, 2])
                    p.wait(nc.sync, "dve")
                    p.mark("ld", nc.sync.dma_start(
                        out=wout, in_=acc_ap), 16)
                    p.record_use("gth", "ld")
                    p.wait(nc.gpsimd, "ld")
                    p.mark("cc", nc.gpsimd.collective_compute(
                        "AllGather", ALU.bypass, replica_groups=RG,
                        ins=[wpc[pc].ap()], outs=[wag[pc].ap()]))

        def emit_ln(xnT):
            base_act = p.val("act")
            for tt in range(NTT):
                xt = x_tm[:, tt * E:(tt + 1) * E]
                p.mark("dve", nc.vector.reduce_sum(ln_t1[:, :], xt, axis=AX.X))
                p.wait(nc.vector, "dve")
                p.mark("dve", nc.vector.tensor_scalar(
                    ln_negm[:, tt:tt + 1], ln_t1[:, :], -1.0 / E, None, ALU.mult))
                p.wait(nc.scalar, "dve")
                p.mark("act", nc.scalar.activation(
                    ln_junk[:, :], xt, AF.Square,
                    bias=ln_negm[:, tt:tt + 1], scale=1.0,
                    accum_out=ln_vs[:, tt:tt + 1]))
            for tt in range(NTT):
                nc.scalar.activation(ln_rs[:, tt:tt + 1], ln_vs[:, tt:tt + 1],
                                     AF.Ln, scale=1.0 / E,
                                     bias=consts[:, C_EPS:C_EPS + 1])
            for tt in range(NTT):
                p.mark("act", nc.scalar.activation(
                    ln_rs[:, tt:tt + 1], ln_rs[:, tt:tt + 1], AF.Exp, scale=-0.5))
            for tt in range(NTT):
                buf = tt % 2
                xt = x_tm[:, tt * E:(tt + 1) * E]
                p.wait(nc.vector, "act", base_act + NTT + 1 + tt)
                p.wait_reuse(nc.vector, ("xn_t", buf))
                p.wait_reuse(nc.vector, "o_stg")
                nc.vector.tensor_scalar(o_stg[:, :], xt, ln_negm[:, tt:tt + 1],
                                        None, ALU.add)
                p.mark("dve", nc.vector.tensor_scalar(
                    xn_tb[buf][:, :], o_stg[:, :], ln_rs[:, tt:tt + 1],
                    None, ALU.mult))
                xn_mark = p.val("dve")
                p.record_use("o_stg", "dve")
                for g in range(4):
                    k = (tt * 4 + g) % 2
                    bank = ps_tr[k]
                    p.wait_reuse(nc.tensor, ("ps_tr", k))
                    p.wait(nc.tensor, "dve", xn_mark)
                    for j in range(4):
                        hc = g * 4 + j
                        mm = nc.tensor.transpose(
                            bank[:, j * 128:(j + 1) * 128],
                            xn_tb[buf][:, hc * 128:(hc + 1) * 128],
                            ident[:, :])
                    p.mark("pe", mm)
                    if g == 3:
                        p.record_use(("xn_t", buf), "pe")
                    p.wait(nc.vector, "pe")
                    for j in range(4):
                        hc = g * 4 + j
                        ins = nc.vector.tensor_copy(
                            xnT[:, hc * TOK + tt * 128: hc * TOK + (tt + 1) * 128],
                            bank[:, j * 128:(j + 1) * 128])
                    p.mark("dve", ins)
                    p.record_use(("ps_tr", k), "dve")

        def emit_ffn(xnT, scale, tag):
            with ExitStack() as fst:
                wg_sb = [fst.enter_context(
                    nc.sbuf_tensor("wg%s%d" % (tag, i), [128, HC * 128], BF16))
                    for i in range(2)]
                wu_sb = [fst.enter_context(
                    nc.sbuf_tensor("wu%s%d" % (tag, i), [128, HC * 128], BF16))
                    for i in range(2)]
                wd_sb = fst.enter_context(
                    nc.sbuf_tensor("wd%s" % tag, [128, OC * 128], BF16))
                sg_sb = fst.enter_context(
                    nc.sbuf_tensor("sg%s" % tag, [128, 512], BF16))
                h_all = fst.enter_context(
                    nc.sbuf_tensor("h%s" % tag, [128, OC * 512], BF16))
                zd_sb = [fst.enter_context(
                    nc.sbuf_tensor("zd%s%d" % (tag, i), [128, 512], BF16))
                    for i in range(2)]

                p.barrier(nc.sync)
                for tc in range(2):
                    for oc in range(OC):
                        pc, j = piece_of(oc)
                        buf = oc % 2
                        p.wait_reuse(nc.sync, ("wg", buf))
                        if tc == 0 and tag == "f1" and j == 0:
                            p.wait(nc.sync, "cc", pc + 1)
                        wg_in = wag[pc].ap()[:, 0, j] \
                            .rearrange("r p t o -> r p (t o)").transpose([1, 0, 2])
                        wu_in = wag[pc].ap()[:, 1, j] \
                            .rearrange("r p t o -> r p (t o)").transpose([1, 0, 2])
                        wsb_ap = wg_sb[buf].ap().rearrange(
                            "p (r to) -> p r to", r=NCORE)
                        wub_ap = wu_sb[buf].ap().rearrange(
                            "p (r to) -> p r to", r=NCORE)
                        p.mark("ld", nc.sync.dma_start(out=wsb_ap, in_=wg_in), 16)
                        p.mark("ld", nc.sync.dma_start(out=wub_ap, in_=wu_in), 16)
                        wld = p.val("ld")
                        p.wait(nc.tensor, "ld", wld)
                        if oc == 0:
                            p.wait(nc.tensor, "dve")
                        p.wait_reuse(nc.tensor, "ps_a")
                        for hcc in range(HC):
                            mm = nc.tensor.matmul(
                                ps_a[:, :],
                                wg_sb[buf][:, hcc * 128:(hcc + 1) * 128],
                                xnT[:, hcc * TOK + tc * 512:
                                    hcc * TOK + tc * 512 + 512],
                                start=(hcc == 0), stop=(hcc == HC - 1))
                        p.mark("pe", mm)
                        gate_mark = p.val("pe")
                        p.wait_reuse(nc.tensor, "ps_b")
                        for hcc in range(HC):
                            mm = nc.tensor.matmul(
                                ps_b[:, :],
                                wu_sb[buf][:, hcc * 128:(hcc + 1) * 128],
                                xnT[:, hcc * TOK + tc * 512:
                                    hcc * TOK + tc * 512 + 512],
                                start=(hcc == 0), stop=(hcc == HC - 1))
                        p.mark("pe", mm)
                        p.record_use(("wg", buf), "pe")
                        p.wait(nc.scalar, "pe", gate_mark)
                        p.mark("act", nc.scalar.activation(
                            sg_sb[:, :], ps_a[:, :], AF.Silu))
                        p.record_use("ps_a", "act")
                        p.wait(nc.vector, "act")
                        p.wait(nc.vector, "pe")
                        p.mark("dve", nc.vector.tensor_tensor(
                            h_all[:, oc * 512:(oc + 1) * 512], sg_sb[:, :],
                            ps_b[:, :], ALU.mult))
                        p.record_use("ps_b", "dve")
                    h_done = p.val("dve")
                    for dc in range(HC):
                        r, t = dc // 2, dc % 2
                        p.wait_reuse(nc.sync, "wd")
                        if tc == 0 and tag == "f1" and dc == 0:
                            p.wait(nc.sync, "cc", 4)
                        for pc in range(4):
                            din = wag[pc].ap()[r, 2, :, :, t].transpose([1, 0, 2])
                            dout = wd_sb.ap()[:, PSTART[pc] * 128:
                                              (PSTART[pc] + PIECES[pc]) * 128] \
                                .rearrange("p (j o) -> p j o", j=PIECES[pc])
                            p.mark("ld", nc.sync.dma_start(out=dout, in_=din), 16)
                        wld = p.val("ld")
                        p.wait(nc.tensor, "ld", wld)
                        p.wait(nc.tensor, "dve", h_done)
                        p.wait_reuse(nc.tensor, ("ps_c", dc % 2))
                        for oc in range(OC):
                            mm = nc.tensor.matmul(
                                ps_c[dc % 2][:, :],
                                wd_sb[:, oc * 128:(oc + 1) * 128],
                                h_all[:, oc * 512:(oc + 1) * 512],
                                start=(oc == 0), stop=(oc == OC - 1))
                        p.mark("pe", mm)
                        p.record_use("wd", "pe")
                        zmark = p.val("pe")
                        p.wait(nc.vector, "pe", zmark)
                        p.wait_reuse(nc.vector, ("zd", dc % 2))
                        p.mark("dve", nc.vector.tensor_copy(
                            zd_sb[dc % 2][:, :], ps_c[dc % 2][:, :]))
                        p.record_use(("ps_c", dc % 2), "dve")
                        zd_mark = p.val("dve")
                        p.wait(nc.tensor, "dve", zd_mark)
                        p.wait_reuse(nc.tensor, "ps_zt")
                        for j4 in range(4):
                            mm = nc.tensor.transpose(
                                ps_zt[:, j4 * 128:(j4 + 1) * 128],
                                zd_sb[dc % 2][:, j4 * 128:(j4 + 1) * 128],
                                ident[:, :])
                        p.mark("pe", mm)
                        p.record_use(("zd", dc % 2), "pe")
                        p.wait(nc.vector, "pe")
                        for j4 in range(4):
                            xsl = x_tm[:, (tc * 4 + j4) * E + dc * 128:
                                       (tc * 4 + j4) * E + (dc + 1) * 128]
                            ins = nc.vector.scalar_tensor_tensor(
                                xsl, ps_zt[:, j4 * 128:(j4 + 1) * 128], scale, xsl,
                                ALU.mult, ALU.add)
                        p.mark("dve", ins)
                        p.record_use("ps_zt", "dve")

        def emit_qkv(xnT):
            p.wait(nc.sync, "dve")
            xn_in = xnT.ap().rearrange("p (hc t) -> p hc t", hc=HC)
            xn_out = xnag_in.ap().rearrange("(hc p) t -> hc p t", p=128) \
                .transpose([1, 0, 2])
            p.mark("ld", nc.sync.dma_start(out=xn_out, in_=xn_in), 16)
            p.wait(nc.gpsimd, "ld")
            p.mark("cc", nc.gpsimd.collective_compute(
                "AllGather", ALU.bypass, replica_groups=RG,
                ins=[xnag_in.ap()], outs=[xnag_out.ap()]))
            cc_xn = p.val("cc")

            with ExitStack() as qst:
                wq_sb = qst.enter_context(
                    nc.sbuf_tensor("wq_sb", [128, HC * 768], BF16))
                xnblk = [qst.enter_context(
                    nc.sbuf_tensor("xnblk%d" % i, [128, HC * 512], BF16))
                    for i in range(2)]
                qk_stage = qst.enter_context(
                    nc.sbuf_tensor("qk_stage", [128, 512], BF16))
                vt_stage = qst.enter_context(
                    nc.sbuf_tensor("vt_stage", [128, 512], BF16))
                p.barrier(nc.gpsimd, nc.sync)
                wq_in = wqkvT_d.ap().rearrange("(hc p) r -> hc p r", p=128) \
                    .transpose([1, 0, 2])
                wq_out = wq_sb.ap().rearrange("p (hc r) -> p hc r", hc=HC)
                p.mark("ld2", nc.gpsimd.dma_start(out=wq_out, in_=wq_in), 16)
                for i in range(16):
                    buf = i % 2
                    p.wait(nc.sync, "cc", cc_xn)
                    p.wait_reuse(nc.sync, ("xnblk", buf))
                    xb_in = xnag_out.ap()[i // 2] \
                        .rearrange("(hc p) t -> hc p t", p=128)[
                            :, :, (i % 2) * 512:(i % 2) * 512 + 512] \
                        .transpose([1, 0, 2])
                    xb_out = xnblk[buf].ap().rearrange("p (hc t) -> p hc t", hc=HC)
                    p.mark("ld", nc.sync.dma_start(out=xb_out, in_=xb_in), 16)
                    wld = p.val("ld")
                    for rc in range(6):
                        ps = ps_a if rc % 2 == 0 else ps_b
                        psk = "ps_a" if rc % 2 == 0 else "ps_b"
                        p.wait(nc.tensor, "ld", wld)
                        if i == 0 and rc == 0:
                            p.wait(nc.tensor, "ld2")
                        p.wait_reuse(nc.tensor, psk)
                        for hc in range(HC):
                            mm = nc.tensor.matmul(
                                ps[:, :],
                                wq_sb[:, hc * 768 + rc * 128:
                                      hc * 768 + (rc + 1) * 128],
                                xnblk[buf][:, hc * 512:(hc + 1) * 512],
                                start=(hc == 0), stop=(hc == HC - 1))
                        p.mark("pe", mm)
                        if rc == 5:
                            p.record_use(("xnblk", buf), "pe")
                        pemark = p.val("pe")
                        if rc < 4:
                            p.wait(nc.vector, "pe", pemark)
                            p.wait_reuse(nc.vector, "qk_stage")
                            p.mark("dve", nc.vector.tensor_copy(
                                qk_stage[:, :], ps[:, :]))
                            p.record_use(psk, "dve")
                            p.wait(nc.sync, "dve")
                            dst = (qspill if rc < 2 else kspill).ap()[
                                rc % 2, :, i * 512:(i + 1) * 512]
                            p.mark("ld", nc.sync.dma_start(
                                out=dst, in_=qk_stage[:, :]), 16)
                            p.record_use("qk_stage", "ld")
                        else:
                            p.wait(nc.vector, "pe", pemark)
                            p.wait_reuse(nc.vector, "qk_stage")
                            p.mark("dve", nc.vector.tensor_copy(
                                qk_stage[:, :], ps[:, :]))
                            p.record_use(psk, "dve")
                            vmark = p.val("dve")
                            p.wait(nc.tensor, "dve", vmark)
                            p.wait_reuse(nc.tensor, "ps_zt")
                            for j4 in range(4):
                                mm = nc.tensor.transpose(
                                    ps_zt[:, j4 * 128:(j4 + 1) * 128],
                                    qk_stage[:, j4 * 128:(j4 + 1) * 128],
                                    ident[:, :])
                            p.mark("pe", mm)
                            p.record_use("qk_stage", "pe")
                            p.wait(nc.vector, "pe")
                            p.wait_reuse(nc.vector, "vt_stage")
                            p.mark("dve", nc.vector.tensor_copy(
                                vt_stage[:, :], ps_zt[:, :]))
                            p.record_use("ps_zt", "dve")
                            p.wait(nc.sync, "dve")
                            vdst = vtspill.ap()[rc - 4,
                                                i * 512:(i + 1) * 512, :] \
                                .rearrange("(j p) d -> j p d", p=128) \
                                .transpose([1, 0, 2])
                            vsrc = vt_stage.ap().rearrange(
                                "p (j d) -> p j d", j=4)
                            p.mark("ld", nc.sync.dma_start(
                                out=vdst, in_=vsrc), 16)
                            p.record_use("vt_stage", "ld")

        def emit_attention():
            with ExitStack() as ast:
                qh = ast.enter_context(nc.sbuf_tensor("qh", [128, S], BF16))
                kh = ast.enter_context(nc.sbuf_tensor("kh", [128, S], BF16))
                vt_sb = ast.enter_context(nc.sbuf_tensor("vt_sb", [128, S], BF16))
                swp = ast.enter_context(nc.sbuf_tensor("swp", [64, S], BF16))
                cs_cos = ast.enter_context(nc.sbuf_tensor("cs_cos", [64, S], BF16))
                cs_sin = ast.enter_context(nc.sbuf_tensor("cs_sin", [64, S], BF16))
                masks_sb = ast.enter_context(
                    nc.sbuf_tensor("masks_sb", [128, 8 * 512], F32))
                pt_sb = [ast.enter_context(
                    nc.sbuf_tensor("pt_sb%d" % i, [128, 512], BF16))
                    for i in range(2)]
                scr_P = ast.enter_context(nc.sbuf_tensor("scr_P", [128, 512], BF16))
                sums = ast.enter_context(nc.sbuf_tensor("sums", [128, 16], F32))
                sum4 = ast.enter_context(nc.sbuf_tensor("sum4", [128, 4], F32))
                m4_sb = ast.enter_context(nc.sbuf_tensor("m4_sb", [128, 512], F32))
                rbc_sb = ast.enter_context(nc.sbuf_tensor("rbc_sb", [128, 512], F32))
                av_sb = ast.enter_context(nc.sbuf_tensor("av_sb", [128, 512], BF16))

                p.barrier(nc.sync, nc.gpsimd)
                p.mark("ld2", nc.gpsimd.dma_start(out=cs_cos[:, :],
                                                  in_=cossin_d.ap()[0]), 16)
                p.mark("ld2", nc.gpsimd.dma_start(out=cs_sin[:, :],
                                                  in_=cossin_d.ap()[1]), 16)
                m_in = masks_d.ap().transpose([1, 0, 2])
                m_out = masks_sb.ap().rearrange("p (m f) -> p m f", m=8)
                p.mark("ld", nc.sync.dma_start(out=m_out, in_=m_in), 16)

                for h in range(2):
                    p.wait_reuse(nc.sync, "qh")
                    p.mark("ld", nc.sync.dma_start(
                        out=qh[:, :], in_=qspill.ap()[h]), 16)
                    p.mark("ld", nc.sync.dma_start(
                        out=kh[:, :], in_=kspill.ap()[h]), 16)
                    vt_in = vtspill.ap()[h].rearrange("(c p) d -> c p d", p=128) \
                        .transpose([1, 0, 2])
                    vt_out = vt_sb.ap().rearrange("p (c d) -> p c d", c=64)
                    p.mark("ld", nc.sync.dma_start(out=vt_out, in_=vt_in), 16)
                    qkld = p.val("ld")

                    for t_sb in (qh, kh):
                        p.wait(nc.sync, "ld", qkld)
                        p.wait_reuse(nc.sync, "swp")
                        d1 = nc.sync.dma_start(out=swp.ap()[0:63:2, :],
                                               in_=t_sb.ap()[1:64:2, :])
                        p.sem["ld"].inc(d1, 16)
                        p.mark("ld", nc.sync.dma_start(
                            out=swp.ap()[1:64:2, :],
                            in_=t_sb.ap()[0:63:2, :]), 16)
                        p.wait(nc.vector, "ld")
                        p.wait(nc.vector, "ld2")
                        nc.vector.tensor_tensor(swp[:, :], swp[:, :],
                                                cs_sin[:, :], ALU.mult)
                        nc.vector.tensor_tensor(t_sb[0:64, :], t_sb[0:64, :],
                                                cs_cos[:, :], ALU.mult)
                        ins = nc.vector.tensor_tensor(
                            t_sb[0:64, :], t_sb[0:64, :], swp[:, :], ALU.add)
                        p.mark("dve", ins)
                        p.record_use("swp", "dve")
                    p.mark("dve", nc.vector.tensor_scalar(
                        qh[:, :], qh[:, :], 1.0 / np.sqrt(HD), None, ALU.mult))
                    rope_mark = p.val("dve")

                    for stt in range(NST):
                        nkc = stt + 1
                        for jq in range(4):
                            q0 = stt * 512 + jq * 128
                            p.wait_reuse(nc.scalar, "sums")
                            for kc in range(nkc):
                                ps = ps_a if kc % 2 == 0 else ps_b
                                psk = "ps_a" if kc % 2 == 0 else "ps_b"
                                p.wait(nc.tensor, "dve", rope_mark)
                                p.wait_reuse(nc.tensor, psk)
                                mm = nc.tensor.matmul(
                                    ps[:, :], qh[:, q0:q0 + 128],
                                    kh[:, kc * 512:(kc + 1) * 512],
                                    start=True, stop=True)
                                p.mark("pe", mm)
                                if kc == stt:
                                    p.wait(nc.vector, "pe")
                                    p.mark("dve", nc.vector.tensor_tensor(
                                        ps[:, :], ps[:, :],
                                        masks_sb[:, jq * 512:(jq + 1) * 512],
                                        ALU.add))
                                    p.wait(nc.scalar, "dve")
                                else:
                                    p.wait(nc.scalar, "pe")
                                p.mark("act", nc.scalar.activation(
                                    scr_P[:, :], ps[:, :], AF.Exp,
                                    bias=consts[:, C_EB:C_EB + 1],
                                    accum_out=sums[:, kc:kc + 1]))
                                p.record_use(psk, "act")
                            p.wait(nc.vector, "act")
                            p.mark("dve", nc.vector.reduce_sum(
                                sum4[:, jq:jq + 1], sums[:, 0:nkc], axis=AX.X))
                            p.record_use("sums", "dve")
                        p.wait(nc.vector, "dve")
                        p.mark("dve", nc.vector.reciprocal(sum4[:, :], sum4[:, :]))
                        p.wait(nc.vector, "dve")
                        for jq in range(4):
                            ins = nc.vector.tensor_scalar(
                                m4_sb[:, jq * 128:(jq + 1) * 128],
                                ones128[:, :], sum4[:, jq:jq + 1], None, ALU.mult)
                        p.mark("dve", ins)
                        m4_mark = p.val("dve")
                        for kc in range(nkc):
                            for jk in range(4):
                                ps = ps_c[jk % 2]
                                psk = ("ps_c", jk % 2)
                                k0 = kc * 512 + jk * 128
                                p.wait_reuse(nc.tensor, psk)
                                mm = nc.tensor.matmul(
                                    ps[:, :], kh[:, k0:k0 + 128],
                                    qh[:, stt * 512:(stt + 1) * 512],
                                    start=True, stop=True)
                                p.mark("pe", mm)
                                if kc == stt:
                                    p.wait(nc.vector, "pe")
                                    p.mark("dve", nc.vector.tensor_tensor(
                                        ps[:, :], ps[:, :],
                                        masks_sb[:, (4 + jk) * 512:
                                                 (5 + jk) * 512],
                                        ALU.add))
                                    p.wait(nc.scalar, "dve")
                                else:
                                    p.wait(nc.scalar, "pe")
                                pbuf = pt_sb[jk % 2]
                                p.wait_reuse(nc.scalar, ("pt", jk % 2))
                                p.mark("act", nc.scalar.activation(
                                    pbuf[:, :], ps[:, :], AF.Exp,
                                    bias=consts[:, C_EB:C_EB + 1]))
                                p.record_use(psk, "act")
                                ptmark = p.val("act")
                                p.wait(nc.tensor, "act", ptmark)
                                if kc == 0 and jk == 0:
                                    p.wait_reuse(nc.tensor, "ps_av")
                                ci = kc * 4 + jk
                                mm = nc.tensor.matmul(
                                    ps_av[:, :],
                                    vt_sb[:, ci * 128:(ci + 1) * 128],
                                    pbuf[:, :],
                                    start=(ci == 0),
                                    stop=(kc == stt and jk == 3))
                                p.mark("pe", mm)
                                p.record_use(("pt", jk % 2), "pe")
                        av_mark = p.val("pe")
                        p.wait(nc.tensor, "dve", m4_mark)
                        p.wait_reuse(nc.tensor, ("ps_c", 0))
                        p.wait_reuse(nc.tensor, ("ps_c", 1))
                        for jq in range(4):
                            mm = nc.tensor.transpose(
                                ps_c[0][:, jq * 128:(jq + 1) * 128],
                                m4_sb[:, jq * 128:(jq + 1) * 128],
                                identf[:, :])
                        p.mark("pe", mm)
                        rbt_mark = p.val("pe")
                        p.wait(nc.vector, "pe", rbt_mark)
                        p.wait_reuse(nc.vector, "rbc")
                        p.mark("dve", nc.vector.tensor_copy(
                            rbc_sb[:, :], ps_c[0][:, :]))
                        p.record_use(("ps_c", 0), "dve")
                        p.wait(nc.vector, "pe", av_mark)
                        p.wait_reuse(nc.vector, "av_sb")
                        p.mark("dve", nc.vector.tensor_tensor(
                            av_sb[:, :], ps_av[:, :], rbc_sb[:, :], ALU.mult))
                        p.record_use("ps_av", "dve")
                        p.record_use("rbc", "dve")
                        p.wait(nc.sync, "dve")
                        dst = avag_in.ap()[stt // 2, h * 128:(h + 1) * 128,
                                           (stt % 2) * 512:(stt % 2) * 512 + 512]
                        p.mark("ld", nc.sync.dma_start(
                            out=dst, in_=av_sb[:, :]), 16)
                        p.record_use("av_sb", "ld")
                    p.record_use("qh", "pe")

            p.wait(nc.gpsimd, "ld")
            p.mark("cc", nc.gpsimd.collective_compute(
                "AllGather", ALU.bypass, replica_groups=RG,
                ins=[avag_in.ap()], outs=[avag_out.ap()]))

        def emit_outproj():
            cc_a2a = p.val("cc")
            with ExitStack() as ost:
                av_all = ost.enter_context(
                    nc.sbuf_tensor("av_all", [128, NH * TOK], BF16))
                wout_sb = ost.enter_context(
                    nc.sbuf_tensor("wout_sb", [128, HC * E], BF16))
                p.barrier(nc.sync, nc.gpsimd)
                p.wait(nc.sync, "cc", cc_a2a)
                rank = nc.sync.partition_id()
                for hl in range(2):
                    av_in = avag_out.ap()[:, bass.ts(rank, 1)][
                        :, :, hl * 128:(hl + 1) * 128, :] \
                        .rearrange("r o p t -> (r o) p t").transpose([1, 0, 2])
                    av_out = av_all.ap().rearrange(
                        "p (r g2 t) -> p r g2 t", r=NCORE, g2=2)[:, :, hl, :]
                    p.mark("ld", nc.sync.dma_start(out=av_out, in_=av_in), 16)
                w_in = woutT_d.ap().rearrange("(dc p) e -> dc p e", p=128) \
                    .transpose([1, 0, 2])
                w_out = wout_sb.ap().rearrange("p (dc e) -> p dc e", dc=HC)
                p.mark("ld2", nc.gpsimd.dma_start(out=w_out, in_=w_in), 16)
                ldv = p.val("ld")
                ld2v = p.val("ld2")
                for ts in range(NTT):
                    for ec in range(4):
                        ps = ps_a if ec % 2 == 0 else ps_b
                        psk = "ps_a" if ec % 2 == 0 else "ps_b"
                        p.wait(nc.tensor, "ld", ldv)
                        p.wait(nc.tensor, "ld2", ld2v)
                        p.wait_reuse(nc.tensor, psk)
                        for g in range(NH):
                            mm = nc.tensor.matmul(
                                ps[:, :],
                                av_all[:, g * TOK + ts * 128:
                                       g * TOK + (ts + 1) * 128],
                                wout_sb[:, g * E + ec * 512:
                                        g * E + ec * 512 + 512],
                                start=(g == 0), stop=(g == NH - 1))
                        p.mark("pe", mm)
                        p.wait(nc.vector, "pe")
                        xsl = x_tm[:, ts * E + ec * 512: ts * E + ec * 512 + 512]
                        p.mark("dve", nc.vector.tensor_tensor(
                            xsl, xsl, ps[:, :], ALU.add))
                        p.record_use(psk, "dve")

        def emit_out_x():
            p.wait(nc.sync, "dve")
            xin = x_tm.ap().rearrange("p (tt d) -> p tt d", tt=NTT)
            oap = out_d.ap().rearrange("(tt p) d -> tt p d", p=128) \
                .transpose([1, 0, 2])
            p.mark("out", nc.sync.dma_start(out=oap, in_=xin), 16)
            p.wait(nc.sync, "out")

        def build(eng):
            emit_prologue()
            emit_gather()
            p.wait(nc.vector, "ld")
            p.wait(nc.scalar, "ld")
            with ExitStack() as s1:
                xnT1 = s1.enter_context(
                    nc.sbuf_tensor("xnT1", [128, HC * TOK], BF16))
                emit_ln(xnT1)
                emit_ffn(xnT1, 0.5, "f1")
            if STAGE <= 3:
                emit_out_x()
                return
            with ExitStack() as s2:
                xnT2 = s2.enter_context(
                    nc.sbuf_tensor("xnT2", [128, HC * TOK], BF16))
                p.wait(nc.vector, "pe")
                emit_ln(xnT2)
                emit_qkv(xnT2)
            emit_attention()
            emit_outproj()
            if STAGE <= 5:
                emit_out_x()
                return
            with ExitStack() as s3:
                xnT3 = s3.enter_context(
                    nc.sbuf_tensor("xnT3", [128, HC * TOK], BF16))
                p.wait(nc.vector, "pe")
                emit_ln(xnT3)
                emit_ffn(xnT3, 0.5, "f3")
            emit_out_x()

        @blk.sync
        def _(eng):
            build(eng)

    return nc


def _host_prep(L0_attn_hidden_states, L0_attn_w_qkv, L0_attn_w_out,
               L0_ffn_lut, L0_ffn_gate_idx, L0_ffn_up_idx, L0_ffn_down_idx):
    x = np.ascontiguousarray(
        np.asarray(L0_attn_hidden_states, np.float32).reshape(S, E))
    lut = np.ascontiguousarray(np.broadcast_to(
        np.asarray(L0_ffn_lut, np.float32).reshape(1, NLUT), (128, NLUT)))
    gT = np.ascontiguousarray(np.asarray(L0_ffn_gate_idx, np.int32).T)
    uT = np.ascontiguousarray(np.asarray(L0_ffn_up_idx, np.int32).T)
    dT = np.ascontiguousarray(np.asarray(L0_ffn_down_idx, np.int32).T)
    wqkv = np.asarray(L0_attn_w_qkv, np.float32)
    woutT = np.ascontiguousarray(np.asarray(L0_attn_w_out, np.float32).T)

    inv = 10000.0 ** (-np.arange(0, ROT, 2, dtype=np.float64) / ROT)
    ang = np.arange(S, dtype=np.float64)[:, None] * inv[None, :]
    cosv = np.cos(ang).T
    sinv = np.sin(ang).T
    cossin = np.zeros((2, 64, S), np.float32)
    cossin[0, 0::2] = cosv
    cossin[0, 1::2] = cosv
    cossin[1, 0::2] = -sinv
    cossin[1, 1::2] = sinv

    masks = np.zeros((8, 128, 512), np.float32)
    pq = np.arange(128)[:, None]
    f = np.arange(512)[None, :]
    for j in range(4):
        masks[j] = np.where(f <= 128 * j + pq, 0.0, -1e9)
        masks[4 + j] = np.where(f >= 128 * j + pq, 0.0, -1e9)
    return x, lut, gT, uT, dT, wqkv, woutT, cossin, masks


def kernel(L0_attn_hidden_states, L0_attn_w_qkv, L0_attn_w_out,
           L0_ffn_lut, L0_ffn_gate_idx, L0_ffn_up_idx, L0_ffn_down_idx):
    x, lut, gT, uT, dT, wqkv, woutT, cossin, masks = _host_prep(
        L0_attn_hidden_states, L0_attn_w_qkv, L0_attn_w_out,
        L0_ffn_lut, L0_ffn_gate_idx, L0_ffn_up_idx, L0_ffn_down_idx)

    nc = build_program()
    in_maps = []
    for c in range(NCORE):
        rows = []
        for kind_off in (0, 1024, 512):
            for h in (2 * c, 2 * c + 1):
                base = 1536 * (h // 4) + kind_off + 128 * (h % 4)
                rows.extend(range(base, base + 128))
        wq_c = np.ascontiguousarray(wqkv[rows].T)
        m = {
            "x": np.ascontiguousarray(x[c * TOK:(c + 1) * TOK]),
            "lut": lut,
            "gidxT": np.ascontiguousarray(gT[256 * c:256 * (c + 1)]),
            "uidxT": np.ascontiguousarray(uT[256 * c:256 * (c + 1)]),
            "didxT": np.ascontiguousarray(dT[:, 256 * c:256 * (c + 1)]),
            "wqkvT": wq_c,
            "woutT": woutT,
            "cossin": cossin,
            "masks": masks,
        }
        in_maps.append(m)
    res = run_bass_kernel_spmd(nc, in_maps, core_ids=list(range(NCORE)))
    out = np.concatenate([res.results[c]["out"] for c in range(NCORE)], axis=0)
    return out.reshape(1, S, E)


# version 04
# speedup vs baseline: 1.4159x; 1.4159x over previous
import numpy as np
import concourse.bass as bass
import concourse.mybir as mybir
from concourse.bass_utils import run_bass_kernel_spmd
from contextlib import ExitStack

F32 = mybir.dt.float32
BF16 = mybir.dt.bfloat16
I32 = mybir.dt.int32
AF = mybir.ActivationFunctionType
ALU = mybir.AluOpType
AX = mybir.AxisListType

NCORE = 8
S = 8192
E = 2048
NH = 16
HD = 128
ROT = 64
INTER = 5504
OC = 43
TOK = S // NCORE
NTT = TOK // 128
HC = E // 128
NST = S // 512
PIECES = [11, 11, 11, 10]
PSTART = [0, 11, 22, 33]
NLUT = 16
EXP_BIAS = -30.0
RG = [list(range(NCORE))]

STAGE = 6


def piece_of(oc):
    p = min(oc // 11, 3)
    return p, oc - PSTART[p]


class Ctr:
    def __init__(self, stack, nc, name):
        self.h = stack.enter_context(nc.semaphore(name))
        self.v = 0

    def inc(self, instr, n=1):
        instr.then_inc(self.h, n)
        self.v += n
        return instr


class Prog:

    def __init__(self, nc, stack):
        self.nc = nc
        self.sem = {}
        for name in ("ld", "ld2", "pe", "act", "dve", "cc", "out", "pl"):
            self.sem[name] = Ctr(stack, nc, "s_" + name)
        self.war = {}

    def wait(self, eng, sem_name, val=None):
        s = self.sem[sem_name]
        v = s.v if val is None else val
        if v > 0:
            eng.wait_ge(s.h, v)

    def mark(self, sem_name, instr, n=1):
        return self.sem[sem_name].inc(instr, n)

    def val(self, sem_name):
        return self.sem[sem_name].v

    def record_use(self, buf, sem_name):
        self.war[buf] = (sem_name, self.sem[sem_name].v)

    def wait_reuse(self, eng, buf):
        if buf in self.war:
            sem_name, v = self.war[buf]
            if v > 0:
                eng.wait_ge(self.sem[sem_name].h, v)

    def barrier(self, *engs):
        for eng in engs:
            for sem_name in ("pe", "act", "dve"):
                self.wait(eng, sem_name)


def build_program():
    nc = bass.Bass("TRN2", target_bir_lowering=False)

    x_d = nc.dram_tensor("x", [TOK, E], F32, kind="ExternalInput")
    gidxT_d = nc.dram_tensor("gidxT", [256, INTER], I32, kind="ExternalInput")
    uidxT_d = nc.dram_tensor("uidxT", [256, INTER], I32, kind="ExternalInput")
    didxT_d = nc.dram_tensor("didxT", [INTER, 256], I32, kind="ExternalInput")
    lut_d = nc.dram_tensor("lut", [128, NLUT], F32, kind="ExternalInput")
    wqkvT_d = nc.dram_tensor("wqkvT", [E, 768], F32, kind="ExternalInput")
    woutT_d = nc.dram_tensor("woutT", [E, E], F32, kind="ExternalInput")
    cossin_d = nc.dram_tensor("cossin", [2, 64, S], F32, kind="ExternalInput")
    masks_d = nc.dram_tensor("masks", [8, 128, 512], F32, kind="ExternalInput")
    out_d = nc.dram_tensor("out", [TOK, E], F32, kind="ExternalOutput")

    wpc = [nc.dram_tensor("wpc%d" % pc, [3 * PIECES[pc] * 32768], BF16)
           for pc in range(4)]
    wag = [nc.dram_tensor("wag%d" % pc, [NCORE, 3 * PIECES[pc] * 32768], BF16,
                          addr_space="Shared")
           for pc in range(4)]
    xnag_in = nc.dram_tensor("xnag_in", [E, TOK], BF16)
    xnag_out = nc.dram_tensor("xnag_out", [NCORE, E, TOK], BF16,
                              addr_space="Shared")
    qspill = nc.dram_tensor("qspill", [2, 128, S], BF16)
    kspill = nc.dram_tensor("kspill", [2, 128, S], BF16)
    vtspill = nc.dram_tensor("vtspill", [2, S, 128], BF16)
    avag_in = [nc.dram_tensor("avag_in%d" % h, [NCORE, 128, TOK], BF16)
               for h in range(2)]
    avag_out = [nc.dram_tensor("avag_out%d" % h, [NCORE, NCORE, 128, TOK], BF16,
                               addr_space="Shared") for h in range(2)]

    st = ExitStack()
    with st:
        blk = st.enter_context(nc.Block())
        p = Prog(nc, st)

        x_tm = st.enter_context(nc.sbuf_tensor("x_tm", [128, NTT * E], F32))
        ident = st.enter_context(nc.sbuf_tensor("ident_sb", [128, 128], BF16))
        identf = st.enter_context(nc.sbuf_tensor("identf_sb", [128, 128], F32))
        ones128 = st.enter_context(nc.sbuf_tensor("ones128", [128, 128], F32))
        xn_tb = [st.enter_context(nc.sbuf_tensor("xn_t%d" % i, [128, E], BF16))
                 for i in range(2)]
        ln_junk = st.enter_context(nc.sbuf_tensor("ln_junk", [128, E], BF16))
        ln_negm = st.enter_context(nc.sbuf_tensor("ln_negm", [128, NTT], F32))
        ln_vs = st.enter_context(nc.sbuf_tensor("ln_vs", [128, NTT], F32))
        ln_rs = st.enter_context(nc.sbuf_tensor("ln_rs", [128, NTT], F32))
        ln_t1 = st.enter_context(nc.sbuf_tensor("ln_t1", [128, 1], F32))
        o_stg = st.enter_context(nc.sbuf_tensor("o_stg", [128, E], F32))
        consts = st.enter_context(nc.sbuf_tensor("consts", [128, 2], F32))
        C_EPS, C_EB = 0, 1
        lutb = st.enter_context(nc.sbuf_tensor("lutb", [128, NLUT], F32))

        ps_tr = [st.enter_context(nc.psum_tensor("ps_tr%d" % i, [128, 512], BF16))
                 for i in range(2)]
        ps_a = st.enter_context(nc.psum_tensor("ps_a", [128, 512], F32))
        ps_b = st.enter_context(nc.psum_tensor("ps_b", [128, 512], F32))
        ps_c = [st.enter_context(nc.psum_tensor("ps_c%d" % i, [128, 512], F32))
                for i in range(2)]
        ps_av = st.enter_context(nc.psum_tensor("ps_av", [128, 512], F32))
        ps_zt = st.enter_context(nc.psum_tensor("ps_zt", [128, 512], BF16))

        def emit_prologue():
            nc.vector.memset(consts[:, C_EPS:C_EPS + 1], 1e-5)
            nc.vector.memset(ones128[:, :], 1.0)
            p.mark("dve", nc.vector.memset(consts[:, C_EB:C_EB + 1], EXP_BIAS))
            with nc.sbuf_tensor("iota_i32", [128, 128], I32) as iota_i32:
                p.mark("pl", nc.gpsimd.iota(iota_i32[:, :], pattern=[[1, 128]],
                                            base=0, channel_multiplier=-1))
                p.wait(nc.vector, "pl")
                nc.vector.tensor_scalar(
                    ident[:, :], iota_i32[:, :], 0.0, None, ALU.is_equal)
                p.mark("dve", nc.vector.tensor_scalar(
                    identf[:, :], iota_i32[:, :], 0.0, None, ALU.is_equal))
            p.mark("ld", nc.sync.dma_start(out=lutb[:, :], in_=lut_d.ap()), 16)
            xin = x_d.ap().rearrange("(tt p) d -> tt p d", p=128).transpose([1, 0, 2])
            xout_ap = x_tm.ap().rearrange("p (tt d) -> p tt d", tt=NTT)
            p.mark("ld", nc.sync.dma_start(out=xout_ap, in_=xin), 16)

        def emit_gather():
            with ExitStack() as gst:
                gi = gst.enter_context(nc.sbuf_tensor("gth_i32", [128, 11 * 256], I32))
                gb = gst.enter_context(nc.sbuf_tensor("gth_b", [128, 11 * 256], BF16))
                gacc = gst.enter_context(
                    nc.sbuf_tensor("gth_acc", [128, 11 * 256], BF16))
                gscr = gst.enter_context(
                    nc.sbuf_tensor("gth_scr", [128, 11 * 256], BF16))
                p.wait(nc.vector, "ld")

                def gather_tile(n_el):
                    p.wait(nc.vector, "ld")
                    nc.vector.tensor_copy(gb[:, :n_el], gi[:, :n_el])
                    nc.vector.tensor_scalar(gacc[:, :n_el], gb[:, :n_el],
                                            0.0, lutb[:, 0:1], ALU.is_equal,
                                            ALU.mult)
                    for v in range(1, NLUT):
                        nc.vector.tensor_scalar(gscr[:, :n_el], gb[:, :n_el],
                                                float(v), lutb[:, v:v + 1],
                                                ALU.is_equal, ALU.mult)
                        ins = nc.vector.tensor_tensor(
                            gacc[:, :n_el], gacc[:, :n_el], gscr[:, :n_el], ALU.add)
                    p.mark("dve", ins)

                for pc in range(4):
                    npc = PIECES[pc]
                    o0 = PSTART[pc] * 128
                    o1 = o0 + npc * 128
                    for mat, src in ((0, gidxT_d), (1, uidxT_d)):
                        for ht in range(2):
                            p.wait_reuse(nc.sync, "gth")
                            p.mark("ld", nc.sync.dma_start(
                                out=gi[:, :npc * 128],
                                in_=src.ap()[128 * ht:128 * (ht + 1), o0:o1]), 16)
                            gather_tile(npc * 128)
                            acc_ap = gacc.ap()[:, :npc * 128] \
                                .rearrange("p (c o) -> p c o", c=npc)
                            wout = wpc[pc].ap()[mat * npc * 32768:
                                                (mat + 1) * npc * 32768] \
                                .rearrange("(c p t o) -> c p t o",
                                           c=npc, p=128, t=2)[:, :, ht, :] \
                                .transpose([1, 0, 2])
                            p.wait(nc.sync, "dve")
                            p.mark("ld", nc.sync.dma_start(
                                out=wout, in_=acc_ap), 16)
                            p.record_use("gth", "ld")
                    p.wait_reuse(nc.sync, "gth")
                    din = didxT_d.ap()[o0:o1, :] \
                        .rearrange("(c p) d -> c p d", p=128).transpose([1, 0, 2])
                    gout = gi.ap()[:, :npc * 256].rearrange("p (c d) -> p c d", c=npc)
                    p.mark("ld", nc.sync.dma_start(out=gout, in_=din), 16)
                    gather_tile(npc * 256)
                    dreg = wpc[pc].ap()[2 * npc * 32768: 3 * npc * 32768] \
                        .rearrange("(t p j d) -> t p j d", t=2, p=128, j=npc)
                    p.wait(nc.sync, "dve")
                    for t2 in range(2):
                        acc_ap = gacc.ap()[:, :npc * 256] \
                            .rearrange("p (c t d) -> p c t d", c=npc, t=2)[
                                :, :, t2, :]
                        p.mark("ld", nc.sync.dma_start(
                            out=dreg[t2], in_=acc_ap), 16)
                    p.record_use("gth", "ld")
                    p.wait(nc.gpsimd, "ld")
                    p.mark("cc", nc.gpsimd.collective_compute(
                        "AllGather", ALU.bypass, replica_groups=RG,
                        ins=[wpc[pc].ap()], outs=[wag[pc].ap()]))

        def emit_ln(xnT):
            base_act = p.val("act")
            for tt in range(NTT):
                xt = x_tm[:, tt * E:(tt + 1) * E]
                p.mark("dve", nc.vector.reduce_sum(ln_t1[:, :], xt, axis=AX.X))
                p.wait(nc.vector, "dve")
                p.mark("dve", nc.vector.tensor_scalar(
                    ln_negm[:, tt:tt + 1], ln_t1[:, :], -1.0 / E, None, ALU.mult))
                p.wait(nc.scalar, "dve")
                p.mark("act", nc.scalar.activation(
                    ln_junk[:, :], xt, AF.Square,
                    bias=ln_negm[:, tt:tt + 1], scale=1.0,
                    accum_out=ln_vs[:, tt:tt + 1]))
            for tt in range(NTT):
                nc.scalar.activation(ln_rs[:, tt:tt + 1], ln_vs[:, tt:tt + 1],
                                     AF.Ln, scale=1.0 / E,
                                     bias=consts[:, C_EPS:C_EPS + 1])
            for tt in range(NTT):
                p.mark("act", nc.scalar.activation(
                    ln_rs[:, tt:tt + 1], ln_rs[:, tt:tt + 1], AF.Exp, scale=-0.5))
            for tt in range(NTT):
                buf = tt % 2
                xt = x_tm[:, tt * E:(tt + 1) * E]
                p.wait(nc.vector, "act", base_act + NTT + 1 + tt)
                p.wait_reuse(nc.vector, ("xn_t", buf))
                p.wait_reuse(nc.vector, "o_stg")
                nc.vector.tensor_scalar(o_stg[:, :], xt, ln_negm[:, tt:tt + 1],
                                        None, ALU.add)
                p.mark("dve", nc.vector.tensor_scalar(
                    xn_tb[buf][:, :], o_stg[:, :], ln_rs[:, tt:tt + 1],
                    None, ALU.mult))
                xn_mark = p.val("dve")
                p.record_use("o_stg", "dve")
                for g in range(4):
                    k = (tt * 4 + g) % 2
                    bank = ps_tr[k]
                    p.wait_reuse(nc.tensor, ("ps_tr", k))
                    p.wait(nc.tensor, "dve", xn_mark)
                    for j in range(4):
                        hc = g * 4 + j
                        mm = nc.tensor.transpose(
                            bank[:, j * 128:(j + 1) * 128],
                            xn_tb[buf][:, hc * 128:(hc + 1) * 128],
                            ident[:, :])
                    p.mark("pe", mm)
                    if g == 3:
                        p.record_use(("xn_t", buf), "pe")
                    p.wait(nc.vector, "pe")
                    for j in range(4):
                        hc = g * 4 + j
                        ins = nc.vector.tensor_copy(
                            xnT[:, hc * TOK + tt * 128: hc * TOK + (tt + 1) * 128],
                            bank[:, j * 128:(j + 1) * 128])
                    p.mark("dve", ins)
                    p.record_use(("ps_tr", k), "dve")

        def emit_ffn(xnT, scale, tag):
            with ExitStack() as fst:
                wg_sb = [fst.enter_context(
                    nc.sbuf_tensor("wg%s%d" % (tag, i), [128, HC * 128], BF16))
                    for i in range(2)]
                wu_sb = [fst.enter_context(
                    nc.sbuf_tensor("wu%s%d" % (tag, i), [128, HC * 128], BF16))
                    for i in range(2)]
                wd_sb = [fst.enter_context(
                    nc.sbuf_tensor("wd%s%d" % (tag, i), [128, OC * 128], BF16))
                    for i in range(2)]
                sg_sb = fst.enter_context(
                    nc.sbuf_tensor("sg%s" % tag, [128, 512], BF16))
                h_all = fst.enter_context(
                    nc.sbuf_tensor("h%s" % tag, [128, OC * 512], BF16))
                zd_sb = [fst.enter_context(
                    nc.sbuf_tensor("zd%s%d" % (tag, i), [128, 512], BF16))
                    for i in range(2)]

                p.barrier(nc.sync)
                for tc in range(2):
                    for oc in range(OC):
                        pc, j = piece_of(oc)
                        buf = oc % 2
                        p.wait_reuse(nc.sync, ("wg", buf))
                        if tc == 0 and tag == "f1" and j == 0:
                            p.wait(nc.sync, "cc", pc + 1)
                        npc = PIECES[pc]
                        wg_in = wag[pc].ap()[:, j * 32768:(j + 1) * 32768] \
                            .rearrange("r (p to) -> r p to", p=128) \
                            .transpose([1, 0, 2])
                        wu_in = wag[pc].ap()[:, (npc + j) * 32768:
                                             (npc + j + 1) * 32768] \
                            .rearrange("r (p to) -> r p to", p=128) \
                            .transpose([1, 0, 2])
                        wsb_ap = wg_sb[buf].ap().rearrange(
                            "p (r to) -> p r to", r=NCORE)
                        wub_ap = wu_sb[buf].ap().rearrange(
                            "p (r to) -> p r to", r=NCORE)
                        p.mark("ld", nc.sync.dma_start(out=wsb_ap, in_=wg_in), 16)
                        p.mark("ld", nc.sync.dma_start(out=wub_ap, in_=wu_in), 16)
                        wld = p.val("ld")
                        p.wait(nc.tensor, "ld", wld)
                        if oc == 0:
                            p.wait(nc.tensor, "dve")
                        p.wait_reuse(nc.tensor, "ps_a")
                        for hcc in range(HC):
                            mm = nc.tensor.matmul(
                                ps_a[:, :],
                                wg_sb[buf][:, hcc * 128:(hcc + 1) * 128],
                                xnT[:, hcc * TOK + tc * 512:
                                    hcc * TOK + tc * 512 + 512],
                                start=(hcc == 0), stop=(hcc == HC - 1))
                        p.mark("pe", mm)
                        gate_mark = p.val("pe")
                        p.wait_reuse(nc.tensor, "ps_b")
                        for hcc in range(HC):
                            mm = nc.tensor.matmul(
                                ps_b[:, :],
                                wu_sb[buf][:, hcc * 128:(hcc + 1) * 128],
                                xnT[:, hcc * TOK + tc * 512:
                                    hcc * TOK + tc * 512 + 512],
                                start=(hcc == 0), stop=(hcc == HC - 1))
                        p.mark("pe", mm)
                        p.record_use(("wg", buf), "pe")
                        p.wait(nc.scalar, "pe", gate_mark)
                        p.mark("act", nc.scalar.activation(
                            sg_sb[:, :], ps_a[:, :], AF.Silu))
                        p.record_use("ps_a", "act")
                        p.wait(nc.vector, "act")
                        p.wait(nc.vector, "pe")
                        p.mark("dve", nc.vector.tensor_tensor(
                            h_all[:, oc * 512:(oc + 1) * 512], sg_sb[:, :],
                            ps_b[:, :], ALU.mult))
                        p.record_use("ps_b", "dve")
                    h_done = p.val("dve")
                    for dc in range(HC):
                        r, t = dc // 2, dc % 2
                        wbuf = dc % 2
                        p.wait_reuse(nc.sync, ("wd", wbuf))
                        if tc == 0 and tag == "f1" and dc == 0:
                            p.wait(nc.sync, "cc", 4)
                        for pc in range(4):
                            npc = PIECES[pc]
                            d0 = 2 * npc * 32768 + t * npc * 16384
                            din = wag[pc].ap()[r, d0: d0 + npc * 16384] \
                                .rearrange("(p j d) -> p j d", p=128, j=npc)
                            dout = wd_sb[wbuf].ap()[:, PSTART[pc] * 128:
                                                    (PSTART[pc] + npc) * 128] \
                                .rearrange("p (j o) -> p j o", j=npc)
                            p.mark("ld", nc.sync.dma_start(out=dout, in_=din), 16)
                        wld = p.val("ld")
                        p.wait(nc.tensor, "ld", wld)
                        p.wait(nc.tensor, "dve", h_done)
                        p.wait_reuse(nc.tensor, ("ps_c", dc % 2))
                        for oc in range(OC):
                            mm = nc.tensor.matmul(
                                ps_c[dc % 2][:, :],
                                wd_sb[wbuf][:, oc * 128:(oc + 1) * 128],
                                h_all[:, oc * 512:(oc + 1) * 512],
                                start=(oc == 0), stop=(oc == OC - 1))
                        p.mark("pe", mm)
                        p.record_use(("wd", wbuf), "pe")
                        zmark = p.val("pe")
                        p.wait(nc.vector, "pe", zmark)
                        p.wait_reuse(nc.vector, ("zd", dc % 2))
                        p.mark("dve", nc.vector.tensor_copy(
                            zd_sb[dc % 2][:, :], ps_c[dc % 2][:, :]))
                        p.record_use(("ps_c", dc % 2), "dve")
                        zd_mark = p.val("dve")
                        p.wait(nc.tensor, "dve", zd_mark)
                        p.wait_reuse(nc.tensor, "ps_zt")
                        for j4 in range(4):
                            mm = nc.tensor.transpose(
                                ps_zt[:, j4 * 128:(j4 + 1) * 128],
                                zd_sb[dc % 2][:, j4 * 128:(j4 + 1) * 128],
                                ident[:, :])
                        p.mark("pe", mm)
                        p.record_use(("zd", dc % 2), "pe")
                        p.wait(nc.vector, "pe")
                        for j4 in range(4):
                            xsl = x_tm[:, (tc * 4 + j4) * E + dc * 128:
                                       (tc * 4 + j4) * E + (dc + 1) * 128]
                            ins = nc.vector.scalar_tensor_tensor(
                                xsl, ps_zt[:, j4 * 128:(j4 + 1) * 128], scale, xsl,
                                ALU.mult, ALU.add)
                        p.mark("dve", ins)
                        p.record_use("ps_zt", "dve")

        def emit_qkv(xnT):
            p.wait(nc.sync, "dve")
            xn_in = xnT.ap().rearrange("p (hc t) -> p hc t", hc=HC)
            xn_out = xnag_in.ap().rearrange("(hc p) t -> hc p t", p=128) \
                .transpose([1, 0, 2])
            p.mark("ld", nc.sync.dma_start(out=xn_out, in_=xn_in), 16)
            p.wait(nc.gpsimd, "ld")
            p.mark("cc", nc.gpsimd.collective_compute(
                "AllGather", ALU.bypass, replica_groups=RG,
                ins=[xnag_in.ap()], outs=[xnag_out.ap()]))
            cc_xn = p.val("cc")

            with ExitStack() as qst:
                wq_sb = qst.enter_context(
                    nc.sbuf_tensor("wq_sb", [128, HC * 768], BF16))
                xnblk = [qst.enter_context(
                    nc.sbuf_tensor("xnblk%d" % i, [128, HC * 512], BF16))
                    for i in range(2)]
                qk_stage = qst.enter_context(
                    nc.sbuf_tensor("qk_stage", [128, 512], BF16))
                vt_stage = qst.enter_context(
                    nc.sbuf_tensor("vt_stage", [128, 512], BF16))
                p.barrier(nc.gpsimd, nc.sync)
                wq_in = wqkvT_d.ap().rearrange("(hc p) r -> hc p r", p=128) \
                    .transpose([1, 0, 2])
                wq_out = wq_sb.ap().rearrange("p (hc r) -> p hc r", hc=HC)
                p.mark("ld2", nc.gpsimd.dma_start(out=wq_out, in_=wq_in), 16)
                for i in range(16):
                    buf = i % 2
                    p.wait(nc.sync, "cc", cc_xn)
                    p.wait_reuse(nc.sync, ("xnblk", buf))
                    xb_in = xnag_out.ap()[i // 2] \
                        .rearrange("(hc p) t -> hc p t", p=128)[
                            :, :, (i % 2) * 512:(i % 2) * 512 + 512] \
                        .transpose([1, 0, 2])
                    xb_out = xnblk[buf].ap().rearrange("p (hc t) -> p hc t", hc=HC)
                    p.mark("ld", nc.sync.dma_start(out=xb_out, in_=xb_in), 16)
                    wld = p.val("ld")
                    for rc in range(6):
                        ps = ps_a if rc % 2 == 0 else ps_b
                        psk = "ps_a" if rc % 2 == 0 else "ps_b"
                        p.wait(nc.tensor, "ld", wld)
                        if i == 0 and rc == 0:
                            p.wait(nc.tensor, "ld2")
                        p.wait_reuse(nc.tensor, psk)
                        for hc in range(HC):
                            mm = nc.tensor.matmul(
                                ps[:, :],
                                wq_sb[:, hc * 768 + rc * 128:
                                      hc * 768 + (rc + 1) * 128],
                                xnblk[buf][:, hc * 512:(hc + 1) * 512],
                                start=(hc == 0), stop=(hc == HC - 1))
                        p.mark("pe", mm)
                        if rc == 5:
                            p.record_use(("xnblk", buf), "pe")
                        pemark = p.val("pe")
                        if rc < 4:
                            p.wait(nc.vector, "pe", pemark)
                            p.wait_reuse(nc.vector, "qk_stage")
                            p.mark("dve", nc.vector.tensor_copy(
                                qk_stage[:, :], ps[:, :]))
                            p.record_use(psk, "dve")
                            p.wait(nc.sync, "dve")
                            dst = (qspill if rc < 2 else kspill).ap()[
                                rc % 2, :, i * 512:(i + 1) * 512]
                            p.mark("ld", nc.sync.dma_start(
                                out=dst, in_=qk_stage[:, :]), 16)
                            p.record_use("qk_stage", "ld")
                        else:
                            p.wait(nc.vector, "pe", pemark)
                            p.wait_reuse(nc.vector, "qk_stage")
                            p.mark("dve", nc.vector.tensor_copy(
                                qk_stage[:, :], ps[:, :]))
                            p.record_use(psk, "dve")
                            vmark = p.val("dve")
                            p.wait(nc.tensor, "dve", vmark)
                            p.wait_reuse(nc.tensor, "ps_zt")
                            for j4 in range(4):
                                mm = nc.tensor.transpose(
                                    ps_zt[:, j4 * 128:(j4 + 1) * 128],
                                    qk_stage[:, j4 * 128:(j4 + 1) * 128],
                                    ident[:, :])
                            p.mark("pe", mm)
                            p.record_use("qk_stage", "pe")
                            p.wait(nc.vector, "pe")
                            p.wait_reuse(nc.vector, "vt_stage")
                            p.mark("dve", nc.vector.tensor_copy(
                                vt_stage[:, :], ps_zt[:, :]))
                            p.record_use("ps_zt", "dve")
                            p.wait(nc.sync, "dve")
                            vdst = vtspill.ap()[rc - 4,
                                                i * 512:(i + 1) * 512, :] \
                                .rearrange("(j p) d -> j p d", p=128) \
                                .transpose([1, 0, 2])
                            vsrc = vt_stage.ap().rearrange(
                                "p (j d) -> p j d", j=4)
                            p.mark("ld", nc.sync.dma_start(
                                out=vdst, in_=vsrc), 16)
                            p.record_use("vt_stage", "ld")

        def emit_attention():
            with ExitStack() as ast:
                qh = ast.enter_context(nc.sbuf_tensor("qh", [128, S], BF16))
                kh = ast.enter_context(nc.sbuf_tensor("kh", [128, S], BF16))
                vt_sb = ast.enter_context(nc.sbuf_tensor("vt_sb", [128, S], BF16))
                swp = ast.enter_context(nc.sbuf_tensor("swp", [64, S], BF16))
                cs_cos = ast.enter_context(nc.sbuf_tensor("cs_cos", [64, S], BF16))
                cs_sin = ast.enter_context(nc.sbuf_tensor("cs_sin", [64, S], BF16))
                masks_sb = ast.enter_context(
                    nc.sbuf_tensor("masks_sb", [128, 4 * 512], F32))
                pt_sb = [ast.enter_context(
                    nc.sbuf_tensor("pt_sb%d" % i, [128, 512], BF16))
                    for i in range(2)]
                srow = ast.enter_context(nc.sbuf_tensor("srow", [1, 512], F32))
                rbc_sb = ast.enter_context(nc.sbuf_tensor("rbc_sb", [128, 512], F32))
                av_sb = ast.enter_context(nc.sbuf_tensor("av_sb", [128, 512], BF16))
                onescol = ast.enter_context(nc.sbuf_tensor("onescol", [128, 1], BF16))
                onesrow = ast.enter_context(nc.sbuf_tensor("onesrow", [1, 128], F32))

                p.barrier(nc.sync, nc.gpsimd, nc.vector)
                nc.vector.memset(onescol[:, :], 1.0)
                p.mark("dve", nc.vector.memset(onesrow[:, :], 1.0))
                p.mark("ld2", nc.gpsimd.dma_start(out=cs_cos[:, :],
                                                  in_=cossin_d.ap()[0]), 16)
                p.mark("ld2", nc.gpsimd.dma_start(out=cs_sin[:, :],
                                                  in_=cossin_d.ap()[1]), 16)
                m_in = masks_d.ap()[4:8].transpose([1, 0, 2])
                m_out = masks_sb.ap().rearrange("p (m f) -> p m f", m=4)
                p.mark("ld", nc.sync.dma_start(out=m_out, in_=m_in), 16)

                for h in range(2):
                    p.wait_reuse(nc.sync, "qh")
                    p.mark("ld", nc.sync.dma_start(
                        out=qh[:, :], in_=qspill.ap()[h]), 16)
                    p.mark("ld", nc.sync.dma_start(
                        out=kh[:, :], in_=kspill.ap()[h]), 16)
                    vt_in = vtspill.ap()[h].rearrange("(c p) d -> c p d", p=128) \
                        .transpose([1, 0, 2])
                    vt_out = vt_sb.ap().rearrange("p (c d) -> p c d", c=64)
                    p.mark("ld", nc.sync.dma_start(out=vt_out, in_=vt_in), 16)
                    qkld = p.val("ld")

                    for t_sb in (qh, kh):
                        p.wait(nc.sync, "ld", qkld)
                        p.wait_reuse(nc.sync, "swp")
                        d1 = nc.sync.dma_start(out=swp.ap()[0:63:2, :],
                                               in_=t_sb.ap()[1:64:2, :])
                        p.sem["ld"].inc(d1, 16)
                        p.mark("ld", nc.sync.dma_start(
                            out=swp.ap()[1:64:2, :],
                            in_=t_sb.ap()[0:63:2, :]), 16)
                        p.wait(nc.vector, "ld")
                        p.wait(nc.vector, "ld2")
                        nc.vector.tensor_tensor(swp[:, :], swp[:, :],
                                                cs_sin[:, :], ALU.mult)
                        nc.vector.tensor_tensor(t_sb[0:64, :], t_sb[0:64, :],
                                                cs_cos[:, :], ALU.mult)
                        ins = nc.vector.tensor_tensor(
                            t_sb[0:64, :], t_sb[0:64, :], swp[:, :], ALU.add)
                        p.mark("dve", ins)
                        p.record_use("swp", "dve")
                    p.mark("dve", nc.vector.tensor_scalar(
                        qh[:, :], qh[:, :], 1.0 / np.sqrt(HD), None, ALU.mult))
                    rope_mark = p.val("dve")

                    for stt in range(NST):
                        steps = [(kc, jk) for kc in range(stt + 1)
                                 for jk in range(4)]
                        n = len(steps)
                        avb, avk = ((ps_av, "ps_av") if stt % 2 == 0
                                    else (ps_a, "ps_a"))
                        exp_marks = []

                        def emit_avones(i):
                            kc, jk = steps[i]
                            ci = kc * 4 + jk
                            p.wait(nc.tensor, "act", exp_marks[i])
                            if i == 0:
                                p.wait_reuse(nc.tensor, avk)
                                p.wait_reuse(nc.tensor, "ps_sum")
                            mm = nc.tensor.matmul(
                                avb[:, :], vt_sb[:, ci * 128:(ci + 1) * 128],
                                pt_sb[i % 2][:, :],
                                start=(i == 0), stop=(i == n - 1))
                            mm2 = nc.tensor.matmul(
                                ps_b[0:1, :], onescol[:, :], pt_sb[i % 2][:, :],
                                start=(i == 0), stop=(i == n - 1))
                            p.mark("pe", mm2)
                            p.record_use(("pt", i % 2), "pe")

                        for i, (kc, jk) in enumerate(steps):
                            pcb = ps_c[i % 2]
                            pck = ("ps_c", i % 2)
                            k0 = (kc * 4 + jk) * 128
                            if i == 0:
                                p.wait(nc.tensor, "dve", rope_mark)
                            p.wait_reuse(nc.tensor, pck)
                            mm = nc.tensor.matmul(
                                pcb[:, :], kh[:, k0:k0 + 128],
                                qh[:, stt * 512:(stt + 1) * 512],
                                start=True, stop=True)
                            p.mark("pe", mm)
                            if kc == stt:
                                p.wait(nc.vector, "pe")
                                p.mark("dve", nc.vector.tensor_tensor(
                                    pcb[:, :], pcb[:, :],
                                    masks_sb[:, jk * 512:(jk + 1) * 512],
                                    ALU.add))
                                p.wait(nc.scalar, "dve")
                            else:
                                p.wait(nc.scalar, "pe")
                            p.wait_reuse(nc.scalar, ("pt", i % 2))
                            p.mark("act", nc.scalar.activation(
                                pt_sb[i % 2][:, :], pcb[:, :], AF.Exp,
                                bias=consts[:, C_EB:C_EB + 1]))
                            p.record_use(pck, "act")
                            exp_marks.append(p.val("act"))
                            if i >= 1:
                                emit_avones(i - 1)
                        emit_avones(n - 1)
                        av_stop = p.val("pe")

                        p.wait(nc.vector, "pe", av_stop)
                        p.wait_reuse(nc.vector, "srow")
                        p.mark("dve", nc.vector.tensor_copy(
                            srow[:, :], ps_b[0:1, :]))
                        p.wait(nc.vector, "dve")
                        p.mark("dve", nc.vector.reciprocal(srow[:, :], srow[:, :]))
                        rec_mark = p.val("dve")
                        p.wait(nc.tensor, "dve", rec_mark)
                        mm = nc.tensor.matmul(
                            ps_b[:, :], onesrow[0:1, :], srow[0:1, :],
                            start=True, stop=True)
                        p.mark("pe", mm)
                        p.record_use("srow", "pe")
                        p.wait(nc.vector, "pe")
                        p.wait_reuse(nc.vector, "rbc")
                        p.mark("dve", nc.vector.tensor_copy(
                            rbc_sb[:, :], ps_b[:, :]))
                        p.record_use("ps_sum", "dve")
                        p.wait_reuse(nc.vector, "av_sb")
                        p.mark("dve", nc.vector.tensor_tensor(
                            av_sb[:, :], avb[:, :], rbc_sb[:, :], ALU.mult))
                        p.record_use(avk, "dve")
                        p.record_use("rbc", "dve")
                        p.wait(nc.sync, "dve")
                        dst = avag_in[h].ap()[stt // 2, :,
                                              (stt % 2) * 512:(stt % 2) * 512 + 512]
                        p.mark("ld", nc.sync.dma_start(
                            out=dst, in_=av_sb[:, :]), 16)
                        p.record_use("av_sb", "ld")
                    p.record_use("qh", "pe")
                    p.wait(nc.gpsimd, "ld")
                    p.mark("cc", nc.gpsimd.collective_compute(
                        "AllGather", ALU.bypass, replica_groups=RG,
                        ins=[avag_in[h].ap()], outs=[avag_out[h].ap()]))


        def emit_outproj():
            cc_a2a = p.val("cc")
            with ExitStack() as ost:
                av_all = ost.enter_context(
                    nc.sbuf_tensor("av_all", [128, NH * TOK], BF16))
                wout_sb = ost.enter_context(
                    nc.sbuf_tensor("wout_sb", [128, HC * E], BF16))
                p.barrier(nc.sync, nc.gpsimd)
                p.wait(nc.sync, "cc", cc_a2a)
                rank = nc.sync.partition_id()
                for hl in range(2):
                    av_in = avag_out[hl].ap()[:, bass.ts(rank, 1)] \
                        .rearrange("r o p t -> (r o) p t").transpose([1, 0, 2])
                    av_out = av_all.ap().rearrange(
                        "p (r g2 t) -> p r g2 t", r=NCORE, g2=2)[:, :, hl, :]
                    p.mark("ld", nc.sync.dma_start(out=av_out, in_=av_in), 16)
                w_in = woutT_d.ap().rearrange("(dc p) e -> dc p e", p=128) \
                    .transpose([1, 0, 2])
                w_out = wout_sb.ap().rearrange("p (dc e) -> p dc e", dc=HC)
                p.mark("ld2", nc.gpsimd.dma_start(out=w_out, in_=w_in), 16)
                ldv = p.val("ld")
                ld2v = p.val("ld2")
                for ts in range(NTT):
                    for ec in range(4):
                        ps = ps_a if ec % 2 == 0 else ps_b
                        psk = "ps_a" if ec % 2 == 0 else "ps_b"
                        p.wait(nc.tensor, "ld", ldv)
                        p.wait(nc.tensor, "ld2", ld2v)
                        p.wait_reuse(nc.tensor, psk)
                        for g in range(NH):
                            mm = nc.tensor.matmul(
                                ps[:, :],
                                av_all[:, g * TOK + ts * 128:
                                       g * TOK + (ts + 1) * 128],
                                wout_sb[:, g * E + ec * 512:
                                        g * E + ec * 512 + 512],
                                start=(g == 0), stop=(g == NH - 1))
                        p.mark("pe", mm)
                        p.wait(nc.vector, "pe")
                        xsl = x_tm[:, ts * E + ec * 512: ts * E + ec * 512 + 512]
                        p.mark("dve", nc.vector.tensor_tensor(
                            xsl, xsl, ps[:, :], ALU.add))
                        p.record_use(psk, "dve")


        def emit_out_x():
            p.wait(nc.sync, "dve")
            xin = x_tm.ap().rearrange("p (tt d) -> p tt d", tt=NTT)
            oap = out_d.ap().rearrange("(tt p) d -> tt p d", p=128) \
                .transpose([1, 0, 2])
            p.mark("out", nc.sync.dma_start(out=oap, in_=xin), 16)
            p.wait(nc.sync, "out")

        def build(eng):
            emit_prologue()
            emit_gather()
            p.wait(nc.vector, "ld")
            p.wait(nc.scalar, "ld")
            with ExitStack() as s1:
                xnT1 = s1.enter_context(
                    nc.sbuf_tensor("xnT1", [128, HC * TOK], BF16))
                emit_ln(xnT1)
                emit_ffn(xnT1, 0.5, "f1")
            if STAGE <= 3:
                emit_out_x()
                return
            with ExitStack() as s2:
                xnT2 = s2.enter_context(
                    nc.sbuf_tensor("xnT2", [128, HC * TOK], BF16))
                p.wait(nc.vector, "pe")
                emit_ln(xnT2)
                emit_qkv(xnT2)
            emit_attention()
            emit_outproj()
            if STAGE <= 5:
                emit_out_x()
                return
            with ExitStack() as s3:
                xnT3 = s3.enter_context(
                    nc.sbuf_tensor("xnT3", [128, HC * TOK], BF16))
                p.wait(nc.vector, "pe")
                emit_ln(xnT3)
                emit_ffn(xnT3, 0.5, "f3")
            emit_out_x()

        @blk.sync
        def _(eng):
            build(eng)

    return nc


def _host_prep(L0_attn_hidden_states, L0_attn_w_qkv, L0_attn_w_out,
               L0_ffn_lut, L0_ffn_gate_idx, L0_ffn_up_idx, L0_ffn_down_idx):
    x = np.ascontiguousarray(
        np.asarray(L0_attn_hidden_states, np.float32).reshape(S, E))
    lut = np.ascontiguousarray(np.broadcast_to(
        np.asarray(L0_ffn_lut, np.float32).reshape(1, NLUT), (128, NLUT)))
    gT = np.ascontiguousarray(np.asarray(L0_ffn_gate_idx, np.int32).T)
    uT = np.ascontiguousarray(np.asarray(L0_ffn_up_idx, np.int32).T)
    dT = np.ascontiguousarray(np.asarray(L0_ffn_down_idx, np.int32).T)
    wqkv = np.asarray(L0_attn_w_qkv, np.float32)
    woutT = np.ascontiguousarray(np.asarray(L0_attn_w_out, np.float32).T)

    inv = 10000.0 ** (-np.arange(0, ROT, 2, dtype=np.float64) / ROT)
    ang = np.arange(S, dtype=np.float64)[:, None] * inv[None, :]
    cosv = np.cos(ang).T
    sinv = np.sin(ang).T
    cossin = np.zeros((2, 64, S), np.float32)
    cossin[0, 0::2] = cosv
    cossin[0, 1::2] = cosv
    cossin[1, 0::2] = -sinv
    cossin[1, 1::2] = sinv

    masks = np.zeros((8, 128, 512), np.float32)
    pq = np.arange(128)[:, None]
    f = np.arange(512)[None, :]
    for j in range(4):
        masks[j] = np.where(f <= 128 * j + pq, 0.0, -1e9)
        masks[4 + j] = np.where(f >= 128 * j + pq, 0.0, -1e9)
    return x, lut, gT, uT, dT, wqkv, woutT, cossin, masks


def kernel(L0_attn_hidden_states, L0_attn_w_qkv, L0_attn_w_out,
           L0_ffn_lut, L0_ffn_gate_idx, L0_ffn_up_idx, L0_ffn_down_idx):
    x, lut, gT, uT, dT, wqkv, woutT, cossin, masks = _host_prep(
        L0_attn_hidden_states, L0_attn_w_qkv, L0_attn_w_out,
        L0_ffn_lut, L0_ffn_gate_idx, L0_ffn_up_idx, L0_ffn_down_idx)

    nc = build_program()
    in_maps = []
    for c in range(NCORE):
        rows = []
        for kind_off in (0, 1024, 512):
            for h in (2 * c, 2 * c + 1):
                base = 1536 * (h // 4) + kind_off + 128 * (h % 4)
                rows.extend(range(base, base + 128))
        wq_c = np.ascontiguousarray(wqkv[rows].T)
        m = {
            "x": np.ascontiguousarray(x[c * TOK:(c + 1) * TOK]),
            "lut": lut,
            "gidxT": np.ascontiguousarray(gT[256 * c:256 * (c + 1)]),
            "uidxT": np.ascontiguousarray(uT[256 * c:256 * (c + 1)]),
            "didxT": np.ascontiguousarray(dT[:, 256 * c:256 * (c + 1)]),
            "wqkvT": wq_c,
            "woutT": woutT,
            "cossin": cossin,
            "masks": masks,
        }
        in_maps.append(m)
    res = run_bass_kernel_spmd(nc, in_maps, core_ids=list(range(NCORE)))
    out = np.concatenate([res.results[c]["out"] for c in range(NCORE)], axis=0)
    return out.reshape(1, S, E)
